# version 18
# baseline (speedup 1.0000x reference)
import contextlib
import numpy as np

import concourse.bass as bass
import concourse.mybir as mybir
from concourse.bass_utils import run_bass_kernel_spmd

F32 = mybir.dt.float32
BF16 = mybir.dt.bfloat16
AF = mybir.ActivationFunctionType
ALU = mybir.AluOpType

D = 2048
NT = 2048
B = 8
H = 16
DH = 128
KC = D // 128
CC = NT // 512
SCALE = float(DH) ** -0.5
EPS = 1e-5

ENGINES = ("tensor", "scalar", "vector", "sync", "gpsimd")


class Sched:

    SEMS = ("tq", "sc", "ve", "gq", "gk", "hq", "xt", "wq0", "wq1", "xy0", "xy1")

    def __init__(self):
        self.ops = {e: [] for e in ENGINES}
        self.cnt = {s: 0 for s in self.SEMS}

    def op(self, eng, fn, waits=(), incs=()):
        for s, v in waits:
            assert v <= self.cnt[s], f"wait on future value {s}={v} (now {self.cnt[s]})"
        self.ops[eng].append((tuple(waits), fn, tuple(incs)))
        for s, a in incs:
            self.cnt[s] += a
        return dict(self.cnt)

    def m(self, s):
        return self.cnt[s]


def build_nc():
    nc = bass.Bass("TRN2", target_bir_lowering=False)

    xT = nc.dram_tensor("xT", [D, NT], F32, kind="ExternalInput")
    Wq = nc.dram_tensor("Wq", [D, D], F32, kind="ExternalInput")
    Wkv = nc.dram_tensor("Wkv", [D, 2 * DH], F32, kind="ExternalInput")
    Wout = nc.dram_tensor("Wout", [D, D], F32, kind="ExternalInput")
    wT = nc.dram_tensor("wT", [D, D], F32, kind="ExternalInput")
    scol = nc.dram_tensor("scol", [128, KC], F32, kind="ExternalInput")
    bcol = nc.dram_tensor("bcol", [128, KC], F32, kind="ExternalInput")
    cosT = nc.dram_tensor("cosT", [DH, NT], F32, kind="ExternalInput")
    sinT = nc.dram_tensor("sinT", [DH, NT], F32, kind="ExternalInput")
    perm = nc.dram_tensor("perm", [DH, DH], F32, kind="ExternalInput")
    triu = nc.dram_tensor("triu", [128, 128], F32, kind="ExternalInput")
    onescol = nc.dram_tensor("onescol", [128, 1], F32, kind="ExternalInput")
    onesrow = nc.dram_tensor("onesrow", [1, 128], F32, kind="ExternalInput")
    yT = nc.dram_tensor("yT", [D, NT], F32, kind="ExternalOutput")

    S = Sched()
    mk = {}

    with contextlib.ExitStack() as st:
        def sbuf(name, shape, dt):
            return st.enter_context(nc.sbuf_tensor(name, shape, dt))

        A1 = sbuf("A1", [128, KC * NT], BF16)
        A2 = sbuf("A2", [128, KC * NT], BF16)
        KR = sbuf("KR", [128, NT], BF16)
        VS = sbuf("VS", [128, NT], BF16)
        WKV = sbuf("WKV", [128, KC * 256], BF16)
        WS0 = sbuf("WS0", [128, KC * 128], BF16)
        WS1 = sbuf("WS1", [128, KC * 128], BF16)
        ROW0 = sbuf("ROW0", [128, NT], BF16)
        ROW1 = sbuf("ROW1", [128, NT], BF16)
        COSB = sbuf("COSB", [128, NT], BF16)
        SINB = sbuf("SINB", [128, NT], BF16)
        PRM = sbuf("PRM", [128, 128], BF16)
        TRI = sbuf("TRI", [128, 128], BF16)
        ONC = sbuf("ONC", [128, 1], BF16)
        SCB = sbuf("SCB", [128, KC], F32)
        BCB = sbuf("BCB", [128, KC], F32)
        EB = sbuf("EB", [128, 10 * 512], BF16)
        EB2 = sbuf("EB2", [128, 2 * 512], BF16)
        T0 = sbuf("T0", [128, 512], BF16)
        T1 = sbuf("T1", [128, 512], BF16)
        XSQ = sbuf("XSQ", [128, 2 * NT], BF16)
        RB0 = sbuf("RB0", [128, 512], F32)
        RB1 = sbuf("RB1", [128, 512], F32)
        XY0 = sbuf("XY0", [128, 512], F32)
        XY1 = sbuf("XY1", [128, 512], F32)
        MR = sbuf("MR", [1, 512], F32)
        QR = sbuf("QR", [1, 512], F32)
        VE = sbuf("VE", [1, 512], BF16)
        BR = sbuf("BR", [1, 512], BF16)
        ONRB = sbuf("ONRB", [1, 128], BF16)
        PALL = st.enter_context(nc.psum_tensor("PALL", [128, 4096], F32))
        P = [PALL[:, i * 512:(i + 1) * 512] for i in range(8)]
        sem = {k: st.enter_context(nc.semaphore(k)) for k in Sched.SEMS}

        WSTR = (WS0, WS1)
        RB = (RB0, RB1)
        REC = (VE, BR)
        LNR = (MR, QR)
        XY = (XY0, XY1)

        def emit_fill(n, bank, waits=()):
            for i in range(n):
                S.op("tensor",
                     (lambda b: lambda e: e.matmul(
                         P[b][32:33, 0:128], ONC[:], TRI[:],
                         start=False, stop=True))(bank),
                     waits=(waits if i == 0 else []))

        S.op("gpsimd", lambda g: g.dma_start(ONC[:], onescol[:]), incs=[("gq", 16)])
        S.op("gpsimd", lambda g: g.dma_start(ONRB[:], onesrow[:]), incs=[("gq", 16)])
        S.op("gpsimd", lambda g: g.dma_start(PRM[:], perm[:]), incs=[("gq", 16)])
        S.op("gpsimd", lambda g: g.dma_start(TRI[:], triu[:]), incs=[("gq", 16)])
        mk["consts_g"] = S.m("gq")
        mk["onc_g"] = S.m("gq")
        S.op("sync", lambda s: s.dma_start(SCB[:], scol[:]), incs=[("hq", 16)])
        S.op("sync", lambda s: s.dma_start(BCB[:], bcol[:]), incs=[("hq", 16)])
        mk["consts_s"] = S.m("hq")

        xT_r = xT[:].rearrange("(k p) n -> p k n", p=128)
        A1_r = A1[:].rearrange("p (k n) -> p k n", n=NT)
        xt_groups = [(0, 1), (1, 2), (2, 4), (4, 7), (7, 10), (10, 13), (13, 16)]
        xt_of_chunk = {}
        for gidx, (k0, k1) in enumerate(xt_groups):
            sm = "xt" if gidx % 2 == 0 else "xy0"
            S.op("gpsimd",
                 (lambda a, b: lambda e: e.dma_start(A1_r[:, a:b, :], xT_r[:, a:b, :]))(k0, k1),
                 waits=[(sm, S.m(sm))], incs=[(sm, 16)])
            for k in range(k0, k1):
                xt_of_chunk[k] = (sm, S.m(sm))
        mk["xtA_xy0"] = S.m("xy0")
        S.op("gpsimd", lambda g: g.dma_start(COSB[:], cosT[:]), incs=[("gk", 16)])
        S.op("gpsimd", lambda g: g.dma_start(SINB[:], sinT[:]), incs=[("gk", 16)])
        S.op("gpsimd", lambda g: g.dma_start(
            WKV[:].rearrange("p (k c) -> p k c", c=256),
            Wkv[:].rearrange("(k p) c -> p k c", p=128)), incs=[("gk", 16)])
        mk["consts_k"] = S.m("gk")

        mk_sq = {}
        mk_ssq = {}
        for k in range(KC):
            w_sq = [xt_of_chunk[k]]
            if k >= 2:
                w_sq.append(("tq", mk_ssq[k - 2]))
            if k % 2 == 0:
                S.op("scalar",
                     (lambda kk: lambda e: e.activation(
                         XSQ[:, (kk % 2) * NT:(kk % 2) * NT + NT],
                         A1[:, kk * NT:(kk + 1) * NT], AF.Square))(k),
                     waits=w_sq, incs=[("sc", 1)])
                mk_sq[k] = ("sc", S.m("sc"))
            else:
                S.op("vector",
                     (lambda kk: lambda e: e.tensor_mul(
                         XSQ[:, (kk % 2) * NT:(kk % 2) * NT + NT],
                         A1[:, kk * NT:(kk + 1) * NT],
                         A1[:, kk * NT:(kk + 1) * NT]))(k),
                     waits=w_sq, incs=[("ve", 1)])
                mk_sq[k] = ("ve", S.m("ve"))
            for c in range(CC):
                S.op("tensor",
                     (lambda kk, cc: lambda e: e.matmul(
                         P[cc][0:1, :], ONC[:],
                         A1[:, kk * NT + cc * 512: kk * NT + cc * 512 + 512],
                         start=(kk == 0), stop=(kk == KC - 1)))(k, c),
                     waits=([xt_of_chunk[k], ("gq", mk["onc_g"])] if c == 0 else []))
            for c in range(CC):
                S.op("tensor",
                     (lambda kk, cc: lambda e: e.matmul(
                         P[4 + cc][0:1, :], ONC[:],
                         XSQ[:, (kk % 2) * NT + cc * 512: (kk % 2) * NT + cc * 512 + 512],
                         start=(kk == 0), stop=(kk == KC - 1)))(k, c),
                     waits=([mk_sq[k]] if c == 0 else []),
                     incs=([("tq", 1)] if c == CC - 1 else []))
            mk_ssq[k] = S.m("tq")
        mk["stats1_done"] = S.m("tq")

        def emit_ln_rows_and_normalize(src_arena, dst_arena, mk_norm, tag,
                                       chunk_ready_ve=None, abevac_extra=(),
                                       rows_extra=()):
            stats_tq = S.m("tq")
            for c in range(CC):
                w = [("tq", stats_tq)]
                if c > 0:
                    w.append(("tq", mk[f"bcast_{tag}_{c - 1}"]))
                else:
                    w += list(rows_extra)
                S.op("vector",
                     (lambda cc: lambda e: e.tensor_scalar_mul(
                         MR[:], P[cc][0:1, :], 1.0 / D))(c),
                     waits=w, incs=[("ve", 1)])
                S.op("vector",
                     (lambda cc: lambda e: e.tensor_scalar_mul(
                         QR[:], P[4 + cc][0:1, :], 1.0 / D))(c),
                     waits=[("ve", S.m("ve"))], incs=[("ve", 1)])
                mk[f"rowread_{tag}_{c}"] = S.m("ve")
                S.op("vector", lambda e: e.tensor_mul(BR[:], MR[:], MR[:]),
                     waits=[("ve", S.m("ve"))], incs=[("ve", 1)])
                S.op("vector", lambda e: e.scalar_tensor_tensor(
                    VE[:], QR[:], EPS, BR[:], ALU.add, ALU.subtract),
                     waits=[("ve", S.m("ve"))], incs=[("ve", 1)])
                S.op("scalar", lambda e: e.activation(QR[:], VE[:], AF.Ln),
                     waits=[("ve", S.m("ve"))], incs=[("sc", 1)])
                S.op("scalar", lambda e: e.activation(VE[:], QR[:], AF.Exp, scale=-0.5),
                     waits=[("sc", S.m("sc"))], incs=[("sc", 1)])
                mk[f"rstd_{tag}_{c}"] = S.m("sc")
                S.op("vector", lambda e: e.scalar_tensor_tensor(
                    BR[:], MR[:], -1.0, VE[:], ALU.mult, ALU.mult),
                     waits=[("sc", mk[f"rstd_{tag}_{c}"])], incs=[("ve", 1)])
                mk[f"brow_{tag}_{c}"] = S.m("ve")
                S.op("tensor",
                     (lambda cc: lambda e: e.matmul(
                         P[cc][:], ONRB[:], VE[:], start=True, stop=True))(c),
                     waits=[("sc", mk[f"rstd_{tag}_{c}"]),
                            ("ve", mk[f"rowread_{tag}_{c}"]),
                            ("gq", mk["onc_g"])])
                S.op("tensor",
                     (lambda cc: lambda e: e.matmul(
                         P[4 + cc][:], ONRB[:], BR[:], start=True, stop=True))(c),
                     waits=[("ve", mk[f"brow_{tag}_{c}"])], incs=[("tq", 1)])
                mk[f"bcast_{tag}_{c}"] = S.m("tq")
            mk["bcast_" + tag] = S.m("tq")
            ab_waits = [("tq", mk["bcast_" + tag])] + list(abevac_extra)
            for c in range(CC):
                S.op("scalar",
                     (lambda cc: lambda e: e.copy(
                         ROW0[:, cc * 512:cc * 512 + 512], P[cc][:]))(c),
                     waits=(ab_waits if c == 0 else []), incs=[("sc", 1)])
                S.op("scalar",
                     (lambda cc: lambda e: e.copy(
                         ROW1[:, cc * 512:cc * 512 + 512], P[4 + cc][:]))(c),
                     waits=[], incs=[("sc", 1)])
            mk["abevac_" + tag] = S.m("sc")
            for c in range(CC):
                for k in range(KC):
                    w = [("sc", mk["abevac_" + tag])]
                    if chunk_ready_ve is not None:
                        w.append(("ve", chunk_ready_ve[k]))
                    scr = T0 if k % 2 == 0 else T1
                    S.op("vector",
                         (lambda kk, cc, sc_: lambda e: e.tensor_mul(
                             sc_[:],
                             src_arena[:, kk * NT + cc * 512: kk * NT + cc * 512 + 512],
                             ROW0[:, cc * 512:cc * 512 + 512]))(k, c, scr),
                         waits=w, incs=[("ve", 1)])
                    S.op("vector",
                         (lambda kk, cc, sc_: lambda e: e.tensor_add(
                             dst_arena[:, kk * NT + cc * 512: kk * NT + cc * 512 + 512],
                             sc_[:],
                             ROW1[:, cc * 512:cc * 512 + 512]))(k, c, scr),
                         waits=[("ve", S.m("ve"))], incs=[("ve", 1)])
                mk_norm[c] = S.m("ve")

        mk_n1 = {}
        emit_ln_rows_and_normalize(A1, A2, mk_n1, "ln1")

        def emit_rope(dst, dst_off, c, ebuf_off, rot_bank, rot_tq, extra_t1=()):
            S.op("vector",
                 (lambda off, cc: lambda e: e.tensor_mul(
                     T0[:], EB[:, off:off + 512], COSB[:, cc * 512:cc * 512 + 512]))(ebuf_off, c),
                 waits=[("gk", mk["consts_k"]), ("tq", rot_tq), ("ve", S.m("ve"))],
                 incs=[("ve", 1)])
            S.op("vector",
                 (lambda bk, cc: lambda e: e.tensor_mul(
                     T1[:], P[bk][:], SINB[:, cc * 512:cc * 512 + 512]))(rot_bank, c),
                 waits=[("tq", rot_tq)] + list(extra_t1) + [("ve", S.m("ve"))],
                 incs=[("ve", 1)])
            S.op("vector",
                 (lambda d, off: lambda e: e.tensor_add(
                     d[:, off:off + 512], T0[:], T1[:]))(dst, dst_off),
                 waits=[("ve", S.m("ve"))], incs=[("ve", 1)])
            return S.m("ve")

        mk_kr = {}
        mk_qr = {}
        mk_evac = {}
        rot_marks = []

        def flush_rot(pend):
            kind, hh, cc, dst, dst_off, r, evm = pend
            rw = [("sc", evm)]
            if r >= 2:
                rw.append(("ve", rot_marks[r - 2]))
            S.op("tensor",
                 (lambda rr: lambda e: e.matmul(
                     P[2 + rr % 2][:], PRM[:],
                     EB[:, (rr % 2) * 512:(rr % 2) * 512 + 512],
                     start=True, stop=True))(r),
                 waits=rw, incs=[("tq", 1)])
            rot_tq = S.m("tq")
            extra = [("tq", mk["stats1_done"])] if (kind == "q" and hh == 0 and cc == 0) else []
            m = emit_rope(dst, dst_off, cc, (r % 2) * 512, 2 + r % 2,
                          rot_tq, extra_t1=extra)
            rot_marks.append(m)
            if kind == "k":
                mk_kr[cc] = m
            else:
                mk_qr[(hh, cc)] = m

        pendB = None
        evac_pg = []
        gi = 0
        rgi = 0
        for c in range(CC):
            for kind, idx in [("k", c)] + [("v", jt) for jt in range(4 * c, 4 * c + 4)]:
                par = gi % 2
                if kind == "k":
                    w = [("gk", mk["consts_k"]), ("ve", mk_n1[idx])]
                else:
                    w = [("ve", mk_n1[idx // 4])]
                if gi < 2:
                    w.append(("sc", mk["abevac_ln1"]))
                else:
                    w.append(("sc", evac_pg[gi - 2]))
                for k in range(KC):
                    if kind == "k":
                        fn = (lambda kk, cc, pp: lambda e: e.matmul(
                            P[pp][:], WKV[:, kk * 256:kk * 256 + 128],
                            A2[:, kk * NT + cc * 512: kk * NT + cc * 512 + 512],
                            start=(kk == 0), stop=(kk == KC - 1)))(k, idx, par)
                    else:
                        fn = (lambda kk, jj, pp: lambda e: e.matmul(
                            P[pp][:, 0:128],
                            A2[:, kk * NT + jj * 128: kk * NT + jj * 128 + 128],
                            WKV[:, kk * 256 + 128:kk * 256 + 256],
                            start=(kk == 0), stop=(kk == KC - 1)))(k, idx, par)
                    S.op("tensor", fn, waits=(w if k == 0 else []),
                         incs=([("tq", 1)] if k == KC - 1 else []))
                mk[("proj", kind, idx)] = S.m("tq")
                if pendB is not None:
                    flush_rot(pendB)
                    pendB = None
                if kind == "k":
                    ew = [("tq", mk[("proj", kind, idx)])]
                    if rgi >= 2:
                        ew.append(("ve", rot_marks[rgi - 2]))
                    S.op("scalar",
                         (lambda rr, pp: lambda e: e.copy(
                             EB[:, (rr % 2) * 512:(rr % 2) * 512 + 512], P[pp][:]))(rgi, par),
                         waits=ew, incs=[("sc", 1)])
                    pendB = ("k", None, idx, KR, idx * 512, rgi, S.m("sc"))
                    rgi += 1
                else:
                    S.op("scalar",
                         (lambda jj, pp: lambda e: e.copy(
                             VS[:, jj * 128:jj * 128 + 128], P[pp][:, 0:128]))(idx, par),
                         waits=[("tq", mk[("proj", kind, idx)])], incs=[("sc", 1)])
                evac_pg.append(S.m("sc"))
                gi += 1

        mk_qp = {}
        for h in range(H):
            wbuf = WSTR[h % 2]
            w = [("hq", mk["consts_s"])]
            if h >= 2:
                w.append(("tq", mk_qp[(h - 2, CC - 1)]))
            S.op("gpsimd",
                 (lambda hh, wb: lambda e: e.dma_start(
                     wb[:], Wq[hh * 128:hh * 128 + 128, :]))(h, wbuf),
                 waits=w, incs=[(f"wq{h % 2}", 16)])
            mk[("wq", h)] = S.m(f"wq{h % 2}")
            for c in range(CC):
                par = gi % 2
                w = [(f"wq{h % 2}", mk[("wq", h)]), ("sc", evac_pg[gi - 2])]
                for k in range(KC):
                    S.op("tensor",
                         (lambda kk, hh, cc, pp, wb: lambda e: e.matmul(
                             P[pp][:], wb[:, kk * 128:kk * 128 + 128],
                             A2[:, kk * NT + cc * 512: kk * NT + cc * 512 + 512],
                             start=(kk == 0), stop=(kk == KC - 1)))(k, h, c, par, wbuf),
                         waits=(w if k == 0 else []),
                         incs=([("tq", 1)] if k == KC - 1 else []))
                mk_qp[(h, c)] = S.m("tq")
                if pendB is not None:
                    flush_rot(pendB)
                ew = [("tq", mk_qp[(h, c)]), ("ve", rot_marks[rgi - 2])]
                S.op("scalar",
                     (lambda rr, pp: lambda e: e.copy(
                         EB[:, (rr % 2) * 512:(rr % 2) * 512 + 512], P[pp][:]))(rgi, par),
                     waits=ew, incs=[("sc", 1)])
                mk_evac[("q", h, c)] = S.m("sc")
                evac_pg.append(S.m("sc"))
                pendB = ("q", h, c, A1, h * NT + c * 512, rgi, S.m("sc"))
                rgi += 1
                gi += 1
        flush_rot(pendB)
        pendB = None
        mk["krope_done"] = mk_kr[CC - 1]
        mk["phaseB_t_end"] = S.m("tq")
        mk["rope_all_done"] = S.m("ve")

        mk_expp = {}
        mk_mask = {}
        mk_ereader = {}
        mk_dln = {}
        mk_rbc = {}
        mk_rbcast = {}
        mk_ott = {}

        def emit_sim(h, c, idx, si):
            u0 = max(0, (idx - 4 * c) * 128)
            wdt = 512 - u0
            w = [("ve", mk_qr[(h, c)]), ("ve", mk["krope_done"])]
            if si < 4:
                w.append(("sc", mk_evac[("q", H - 1, CC - 1)]))
                w.append(("ve", mk["rope_all_done"]))
            else:
                w.append(("sc", mk_expp[(si - 4) // 2]))
            S.op("tensor",
                 (lambda ps, hh, cc, jj, uu, ww: lambda e: e.matmul(
                     P[ps][:, uu:512], KR[:, jj * 128:jj * 128 + 128],
                     A1[:, hh * NT + cc * 512 + uu: hh * NT + cc * 512 + uu + ww],
                     start=True, stop=True))(si % 4, h, c, idx, u0, wdt),
                 waits=w, incs=[("tq", 1)])
            mk[("sim", si)] = S.m("tq")

        def emit_expp(h, c, pi, si0):
            ps = si0 % 4
            eb0 = si0 % 10
            u0 = max(0, (2 * pi - 4 * c) * 128)
            w = [("tq", mk[("sim", si0 + 1)])]
            if si0 >= 10:
                w.append(("tq", mk_ereader[si0 - 9]))
            else:
                w.append(("tq", mk["phaseB_t_end"]))
                w.append(("ve", mk["rope_all_done"]))
            S.op("scalar",
                 (lambda sl, ee, uu: lambda e: e.activation(
                     EB[:, ee * 512 + uu:ee * 512 + 1024],
                     PALL[:, sl * 512 + uu:sl * 512 + 1024], AF.Exp, scale=SCALE))(ps, eb0, u0),
                 waits=w, incs=[("sc", 1)])
            mk_expp[si0 // 2] = S.m("sc")
            for si in (si0, si0 + 1):
                idx = si - si0 + (pi * 2)
                if idx >= 4 * c:
                    eb = si % 10
                    uu = (idx - 4 * c) * 128
                    S.op("vector",
                         (lambda ee, u_: lambda e: e.tensor_mul(
                             EB[:, ee * 512 + u_:ee * 512 + u_ + 128],
                             EB[:, ee * 512 + u_:ee * 512 + u_ + 128], TRI[:]))(eb, uu),
                         waits=[("sc", mk_expp[si0 // 2]), ("gq", mk["consts_g"])],
                         incs=[("ve", 1)])
                    mk_mask[si] = S.m("ve")

        e2_state = {"n": 0}
        mk_e2den = {}

        def emit_O(h, c, idx, si, par, g, njt, with_ew):
            u0 = max(0, (idx - 4 * c) * 128)
            wdt = 512 - u0
            eb = si % 10
            w0 = []
            if with_ew:
                w0 += ([("ve", mk_mask[si])] if si in mk_mask
                       else [("sc", mk_expp[si // 2])])
            if idx == 0:
                w0.append(("sc", mk["abevac_ln1"]) if g < 2 else ("ve", mk_ott[g - 2]))
            S.op("tensor",
                 (lambda pp, ee, jj, uu, ww, st_, sp_: lambda e: e.matmul(
                     P[4 + pp][:, uu:512], VS[:, jj * 128:jj * 128 + 128],
                     EB[:, ee * 512 + uu:ee * 512 + 512],
                     start=st_, stop=sp_))(par, eb, idx, u0, wdt, idx == 0, idx == njt - 1),
                 waits=w0, incs=[("tq", 1)])
            mk_ereader[si] = S.m("tq")
            if idx == njt - 1:
                mk[("odone", h, c)] = S.m("tq")

        def emit_den(h, c, idx, si, par, g, njt, first_den, last_den, with_ew):
            u0 = max(0, (idx - 4 * c) * 128)
            wdt = 512 - u0
            eb = si % 10
            w0 = []
            if with_ew:
                w0 += ([("ve", mk_mask[si])] if si in mk_mask
                       else [("sc", mk_expp[si // 2])])
            if first_den:
                w0.append(("sc", mk["abevac_ln1"]) if g < 2 else ("ve", mk_rbc[g - 2]))
            S.op("tensor",
                 (lambda pp, ee, uu, ww, st_, sp_: lambda e: e.matmul(
                     P[6 + pp][0:1, uu:512], ONC[:],
                     EB[:, ee * 512 + uu:ee * 512 + 512],
                     start=st_, stop=sp_))(par, eb, u0, wdt, first_den, last_den),
                 waits=w0,
                 incs=([("tq", 1)] if last_den else []))
            if last_den:
                mk[("dend", h, c)] = S.m("tq")

        def emit_e2(h, c, si0, par, g, first_den):
            n = e2_state["n"]
            slot = n % 2
            e2_state["n"] = n + 1
            w = [("sc", mk_expp[si0 // 2]), ("ve", S.m("ve"))]
            if n >= 2:
                w.append(("tq", mk_e2den[n - 2]))
            eb = si0 % 10
            S.op("vector",
                 (lambda sl, ee: lambda e: e.tensor_add(
                     EB2[:, sl * 512:sl * 512 + 512],
                     EB[:, ee * 512:ee * 512 + 512],
                     EB[:, ee * 512 + 512:ee * 512 + 1024]))(slot, eb),
                 waits=w, incs=[("ve", 1)])
            return (n, slot, S.m("ve"), par, g, first_den, si0)

        def flush_pair_den(pd):
            n, slot, ve_mark, par, g, first_den, si0 = pd
            w = [("ve", ve_mark)]
            if first_den:
                w.append(("sc", mk["abevac_ln1"]) if g < 2 else ("ve", mk_rbc[g - 2]))
            S.op("tensor",
                 (lambda pp, sl, st_: lambda e: e.matmul(
                     P[6 + pp][0:1, :], ONC[:],
                     EB2[:, sl * 512:sl * 512 + 512],
                     start=st_, stop=False))(par, slot, first_den),
                 waits=w, incs=[("tq", 1)])
            mk_e2den[n] = S.m("tq")
            mk_ereader[si0] = mk_e2den[n]
            mk_ereader[si0 + 1] = mk_e2den[n]

        def tail_A(h, c, par, g):
            w = [("tq", mk[("dend", h, c)])]
            if g == 0:
                w.append(("ve", mk["brow_ln1_3"]))
            S.op("scalar",
                 (lambda pp: lambda e: e.activation(
                     LNR[pp][:], P[6 + pp][0:1, :], AF.Ln))(par),
                 waits=w, incs=[("sc", 1)])
            w = [("sc", S.m("sc"))]
            if g >= 2:
                w.append(("tq", mk_rbcast[g - 2]))
            else:
                w.append(("tq", mk["bcast_ln1"]))
            S.op("scalar",
                 (lambda pp: lambda e: e.activation(
                     REC[pp][:], LNR[pp][:], AF.Exp, scale=-1.0))(par),
                 waits=w, incs=[("sc", 1)])
            mk_dln[g] = S.m("sc")
            w = [("sc", mk_dln[g])]
            if g < 2:
                w.append(("sc", mk["abevac_ln1"]))
            S.op("tensor",
                 (lambda pp: lambda e: e.matmul(
                     P[6 + pp][:], ONRB[:], REC[pp][:], start=True, stop=True))(par),
                 waits=w, incs=[("tq", 1)])
            mk_rbcast[g] = S.m("tq")

        def tail_B(h, c, par, g):
            w = [("tq", mk_rbcast[g])]
            if g >= 2:
                w.append(("ve", mk_ott[g - 2]))
            S.op("vector",
                 (lambda pp: lambda e: e.tensor_copy(RB[pp][:], P[6 + pp][:]))(par),
                 waits=w, incs=[("ve", 1)])
            mk_rbc[g] = S.m("ve")
            w = [("tq", mk[("odone", h, c)]), ("ve", mk_rbc[g])]
            if g == 0:
                w.append(("tq", mk["phaseB_t_end"]))
            S.op("vector",
                 (lambda pp, hh, cc: lambda e: e.tensor_mul(
                     A2[:, hh * NT + cc * 512: hh * NT + cc * 512 + 512],
                     RB[pp][:], P[4 + pp][:]))(par, h, c),
                 waits=w, incs=[("ve", 1)])
            mk_ott[g] = S.m("ve")

        tiles = []
        for h in range(H):
            for c in range(CC):
                g = CC * h + c
                njt = 4 * (c + 1)
                for idx in range(njt):
                    tiles.append((h, c, idx, g % 2, g, njt))
        n_t = len(tiles)
        n_p = n_t // 2

        def sim_at(si):
            h, c, idx, par, g, njt = tiles[si]
            emit_sim(h, c, idx, si)

        def expp_at(p):
            h, c, idx, par, g, njt = tiles[2 * p]
            emit_expp(h, c, idx // 2, 2 * p)

        for p in (0, 1, 2, 3, 4):
            sim_at(2 * p)
            sim_at(2 * p + 1)
            if p < 3:
                expp_at(p)
        pendC = None
        pend_den = None
        pend_age = 0
        for pi in range(n_p):
            h, c, idx, par, g, njt = tiles[2 * pi]
            si0 = 2 * pi
            if pi + 3 < n_p:
                expp_at(pi + 3)
            if 2 * pi + 10 < n_t:
                sim_at(2 * pi + 10)
                sim_at(2 * pi + 11)
            if pendC is not None:
                pend_age += 1
                if pend_age == 1:
                    tail_A(*pendC)
                elif pend_age == 2:
                    tail_B(*pendC)
                    pendC = None
            if idx >= 4 * c:
                if pend_den is not None:
                    flush_pair_den(pend_den)
                    pend_den = None
                emit_den(h, c, idx, si0, par, g, njt, idx == 0, False, True)
                emit_O(h, c, idx, si0, par, g, njt, False)
                emit_den(h, c, idx + 1, si0 + 1, par, g, njt, False,
                         idx + 1 == njt - 1, True)
                emit_O(h, c, idx + 1, si0 + 1, par, g, njt, False)
            else:
                if pend_den is not None:
                    flush_pair_den(pend_den)
                emit_O(h, c, idx, si0, par, g, njt, True)
                emit_O(h, c, idx + 1, si0 + 1, par, g, njt, True)
                pend_den = emit_e2(h, c, si0, par, g, idx == 0)
            if idx + 1 == njt - 1:
                pendC = (h, c, par, g)
                pend_age = 0
        tail_A(*pendC)
        tail_B(*pendC)
        pendC = None
        mk["attn_t_end"] = S.m("tq")
        mk["exp_last"] = mk_expp[n_p - 1]

        mk_res = {}
        mk_xt = {}
        mk_wmm = {}
        for dt in range(KC):
            if dt < 2:
                w = [("tq", mk_qp[(H - 1, CC - 1)])]
            else:
                w = [("tq", mk_wmm[(dt - 2, CC - 1)])]
            S.op("gpsimd",
                 (lambda dd, wb: lambda e: e.dma_start(
                     wb[:], Wout[dd * 128:dd * 128 + 128, :]))(dt, WSTR[dt % 2]),
                 waits=w, incs=[(f"wq{dt % 2}", 16)])
            mk[("wo", dt)] = S.m(f"wq{dt % 2}")
            for c in range(CC):
                par = (CC * dt + c) % 4
                gD = CC * dt + c
                w = []
                if gD >= 2:
                    w.append(("ve", mk_res[(dt, c - 2) if c >= 2 else (dt - 1, c + 2)]))
                elif c % 2 == 0:
                    w.append(("xy0", mk["xtA_xy0"]))
                S.op("sync",
                     (lambda dd, cc: lambda e: e.dma_start(
                         XY[cc % 2][:], xT[dd * 128:dd * 128 + 128,
                                           cc * 512:cc * 512 + 512]))(dt, c),
                     waits=w, incs=[(f"xy{c % 2}", 16)])
                mk_xt[(dt, c)] = S.m(f"xy{c % 2}")
                w = [(f"wq{dt % 2}", mk[("wo", dt)]), ("ve", mk_ott[CC * (H - 1) + c])]
                if gD < 4:
                    w.append(("sc", mk["exp_last"]))
                else:
                    w.append(("ve", mk_res[(dt - 1, c)]))
                for hh in range(H):
                    S.op("tensor",
                         (lambda kk, dd, cc, pp, wb: lambda e: e.matmul(
                             P[pp][:], wb[:, kk * 128:kk * 128 + 128],
                             A2[:, kk * NT + cc * 512: kk * NT + cc * 512 + 512],
                             start=(kk == 0), stop=(kk == H - 1)))(hh, dt, c, par, WSTR[dt % 2]),
                         waits=(w if hh == 0 else []),
                         incs=([("tq", 1)] if hh == H - 1 else []))
                mk_wmm[(dt, c)] = S.m("tq")
                w = [("tq", mk_wmm[(dt, c)]), (f"xy{c % 2}", mk_xt[(dt, c)])]
                if gD == 0:
                    w.append(("tq", mk["attn_t_end"]))
                S.op("vector",
                     (lambda dd, cc, pp: lambda e: e.tensor_add(
                         A1[:, dd * NT + cc * 512: dd * NT + cc * 512 + 512],
                         XY[cc % 2][:], P[pp][:]))(dt, c, par),
                     waits=w, incs=[("ve", 1)])
                mk_res[(dt, c)] = S.m("ve")
        mk["phaseD_t_end"] = S.m("tq")

        mk_sq2 = {}
        mk_ssq2 = {}
        for k in range(KC):
            w_sq = [("ve", mk_res[(k, CC - 1)])]
            if k >= 2:
                w_sq.append(("tq", mk_ssq2[k - 2]))
            if k % 2 == 0:
                S.op("scalar",
                     (lambda kk: lambda e: e.activation(
                         XSQ[:, (kk % 2) * NT:(kk % 2) * NT + NT],
                         A1[:, kk * NT:(kk + 1) * NT], AF.Square))(k),
                     waits=w_sq, incs=[("sc", 1)])
                mk_sq2[k] = ("sc", S.m("sc"))
            else:
                S.op("vector",
                     (lambda kk: lambda e: e.tensor_mul(
                         XSQ[:, (kk % 2) * NT:(kk % 2) * NT + NT],
                         A1[:, kk * NT:(kk + 1) * NT],
                         A1[:, kk * NT:(kk + 1) * NT]))(k),
                     waits=w_sq, incs=[("ve", 1)])
                mk_sq2[k] = ("ve", S.m("ve"))
            for c in range(CC):
                w = []
                if k == 0 and c == 0:
                    w = [("ve", mk_res[(KC - 1, CC - 1)])]
                S.op("tensor",
                     (lambda kk, cc: lambda e: e.matmul(
                         P[cc][0:1, :], ONC[:],
                         A1[:, kk * NT + cc * 512: kk * NT + cc * 512 + 512],
                         start=(kk == 0), stop=(kk == KC - 1)))(k, c),
                     waits=w)
            for c in range(CC):
                S.op("tensor",
                     (lambda kk, cc: lambda e: e.matmul(
                         P[4 + cc][0:1, :], ONC[:],
                         XSQ[:, (kk % 2) * NT + cc * 512: (kk % 2) * NT + cc * 512 + 512],
                         start=(kk == 0), stop=(kk == KC - 1)))(k, c),
                     waits=([mk_sq2[k]] if c == 0 else []),
                     incs=([("tq", 1)] if c == CC - 1 else []))
            mk_ssq2[k] = S.m("tq")

        mk_n2 = {}
        emit_ln_rows_and_normalize(
            A1, A2, mk_n2, "ln2",
            abevac_extra=[("ve", mk["rope_all_done"])],
            rows_extra=[("sc", mk_dln[CC * H - 1])])

        mk_yv = {}
        mk_ydma = {}
        mk_gmm = {}
        mk_gact = {}
        for ot in range(KC):
            if ot < 2:
                w = [("tq", mk_wmm[(KC - 1, CC - 1)])]
            else:
                w = [("tq", mk_gmm[(ot - 2, CC - 1)])]
            S.op("gpsimd",
                 (lambda oo, wb: lambda e: e.dma_start(
                     wb[:], wT[oo * 128:oo * 128 + 128, :]))(ot, WSTR[ot % 2]),
                 waits=w, incs=[(f"wq{ot % 2}", 16)])
            mk[("wf", ot)] = S.m(f"wq{ot % 2}")
            for c in range(CC):
                gF = CC * ot + c
                par = gF % 4
                w = [(f"wq{ot % 2}", mk[("wf", ot)]), ("ve", mk_n2[c])]
                if gF >= 4:
                    w.append(("sc", mk_gact[(ot - 1, c)]))
                for k in range(KC):
                    S.op("tensor",
                         (lambda kk, oo, cc, pp, wb: lambda e: e.matmul(
                             P[pp][:], wb[:, kk * 128:kk * 128 + 128],
                             A2[:, kk * NT + cc * 512: kk * NT + cc * 512 + 512],
                             start=(kk == 0), stop=(kk == KC - 1)))(k, ot, c, par, WSTR[ot % 2]),
                         waits=(w if k == 0 else []),
                         incs=([("tq", 1)] if k == KC - 1 else []))
                mk_gmm[(ot, c)] = S.m("tq")
                w = [("tq", mk_gmm[(ot, c)])]
                if gF < 2:
                    w.append(("ve", mk_ott[CC * H - 1]))
                else:
                    w.append(("ve", mk_yv[(ot, c - 2) if c >= 2 else (ot - 1, c + 2)]))
                S.op("scalar",
                     (lambda oo, cc: lambda e: e.activation(
                         RB[cc % 2][:], P[(CC * oo + cc) % 4][:], AF.Identity,
                         bias=BCB[:, oo:oo + 1], scale=SCB[:, oo:oo + 1]))(ot, c),
                     waits=w, incs=[("sc", 1)])
                mk_gact[(ot, c)] = S.m("sc")
                w = [("sc", mk_gact[(ot, c)])]
                if gF >= 2:
                    pc2 = (ot, c - 2) if c >= 2 else (ot - 1, c + 2)
                    w.append((f"xy{pc2[1] % 2}", mk_ydma[pc2]))
                S.op("vector",
                     (lambda oo, cc: lambda e: e.tensor_add(
                         XY[cc % 2][:], RB[cc % 2][:],
                         A1[:, oo * NT + cc * 512: oo * NT + cc * 512 + 512]))(ot, c),
                     waits=w, incs=[("ve", 1)])
                mk_yv[(ot, c)] = S.m("ve")
                S.op("sync",
                     (lambda oo, cc: lambda e: e.dma_start(
                         yT[oo * 128:oo * 128 + 128, cc * 512:cc * 512 + 512],
                         XY[cc % 2][:]))(ot, c),
                     waits=[("ve", mk_yv[(ot, c)])], incs=[(f"xy{c % 2}", 16)])
                mk_ydma[(ot, c)] = S.m(f"xy{c % 2}")

        final_xy0 = S.m("xy0")
        final_xy1 = S.m("xy1")
        S.op("sync", lambda e: e.wait_ge(sem["xy0"], final_xy0))
        S.op("sync", lambda e: e.wait_ge(sem["xy1"], final_xy1))

        with nc.Block() as block:
            def runner(ename):
                def run(eng):
                    last = {}
                    for waits, fn, incs in S.ops[ename]:
                        merged = {}
                        for s, v in waits:
                            merged[s] = max(merged.get(s, 0), v)
                        for s, v in merged.items():
                            if v > 0 and last.get(s, -1) < v:
                                eng.wait_ge(sem[s], v)
                                last[s] = v
                        inst = fn(eng)
                        for s, a in incs:
                            inst.then_inc(sem[s], a)
                return run
            block.tensor(runner("tensor"))
            block.scalar(runner("scalar"))
            block.vector(runner("vector"))
            block.sync(runner("sync"))
            block.gpsimd(runner("gpsimd"))

    return nc


def _host_tables():
    inv_freq = (1.0 / (np.float32(10000.0) **
                       (np.arange(0, DH, 2, dtype=np.float32) / np.float32(DH)))).astype(np.float32)
    freqs = np.arange(NT, dtype=np.float32)[:, None] * inv_freq[None, :]
    ang = np.concatenate([freqs, freqs], axis=1)
    cosT = np.ascontiguousarray(np.cos(ang.astype(np.float64)).astype(np.float32).T)
    sinT = np.ascontiguousarray(np.sin(ang.astype(np.float64)).astype(np.float32).T)
    perm = np.zeros((DH, DH), dtype=np.float32)
    for d in range(DH):
        if d < 64:
            perm[d + 64, d] = -1.0
        else:
            perm[d - 64, d] = 1.0
    r = np.arange(128)
    triu = (r[None, :] >= r[:, None]).astype(np.float32)
    return cosT, sinT, perm, triu


def _pretile(m):
    return np.ascontiguousarray(
        m.reshape(KC, 128, KC, 128).transpose(2, 1, 0, 3).reshape(D, D))


def kernel(L0_attn_x, L0_attn_Wq, L0_attn_Wkv, L0_attn_Wout,
           L0_ffn_weight, L0_ffn_scale_weight, L0_ffn_bias):
    x = np.asarray(L0_attn_x, dtype=np.float32)
    Wq_h = _pretile(np.asarray(L0_attn_Wq, dtype=np.float32))
    Wkv_h = np.ascontiguousarray(np.asarray(L0_attn_Wkv, dtype=np.float32))
    Wout_h = _pretile(np.asarray(L0_attn_Wout, dtype=np.float32))
    wT_h = _pretile(np.asarray(L0_ffn_weight, dtype=np.float32).T)
    scol = np.ascontiguousarray(
        np.asarray(L0_ffn_scale_weight, dtype=np.float32).reshape(KC, 128).T)
    bcol = np.ascontiguousarray(
        np.asarray(L0_ffn_bias, dtype=np.float32).reshape(KC, 128).T)
    cosT_h, sinT_h, perm_h, triu_h = _host_tables()
    onescol = np.ones((128, 1), dtype=np.float32)
    onesrow = np.ones((1, 128), dtype=np.float32)

    nc = build_nc()
    in_maps = []
    for b in range(B):
        in_maps.append({
            "xT": np.ascontiguousarray(x[b].T),
            "Wq": Wq_h, "Wkv": Wkv_h, "Wout": Wout_h, "wT": wT_h,
            "scol": scol, "bcol": bcol,
            "cosT": cosT_h, "sinT": sinT_h, "perm": perm_h, "triu": triu_h,
            "onescol": onescol, "onesrow": onesrow,
        })
    res = run_bass_kernel_spmd(nc, in_maps, core_ids=list(range(B)))
    out = np.empty((B, NT, D), dtype=np.float32)
    for b, r in enumerate(res.results):
        out[b] = r["yT"].T
    return out


# version 19
# speedup vs baseline: 1.1299x; 1.1299x over previous
import contextlib
import numpy as np

import concourse.bass as bass
import concourse.mybir as mybir
from concourse.bass_utils import run_bass_kernel_spmd

F32 = mybir.dt.float32
BF16 = mybir.dt.bfloat16
AF = mybir.ActivationFunctionType
ALU = mybir.AluOpType

D = 2048
NT = 2048
B = 8
H = 16
DH = 128
KC = D // 128
CC = NT // 512
SCALE = float(DH) ** -0.5
EPS = 1e-5

ENGINES = ("tensor", "scalar", "vector", "sync", "gpsimd")


class Sched:

    SEMS = ("tq", "sc", "ve", "gq", "gk", "hq", "xt", "wq0", "wq1", "xy0", "xy1")

    def __init__(self):
        self.ops = {e: [] for e in ENGINES}
        self.cnt = {s: 0 for s in self.SEMS}

    def op(self, eng, fn, waits=(), incs=()):
        for s, v in waits:
            assert v <= self.cnt[s], f"wait on future value {s}={v} (now {self.cnt[s]})"
        self.ops[eng].append((tuple(waits), fn, tuple(incs)))
        for s, a in incs:
            self.cnt[s] += a
        return dict(self.cnt)

    def m(self, s):
        return self.cnt[s]


def build_nc():
    nc = bass.Bass("TRN2", target_bir_lowering=False)

    xT = nc.dram_tensor("xT", [D, NT], F32, kind="ExternalInput")
    Wq = nc.dram_tensor("Wq", [D, D], F32, kind="ExternalInput")
    Wkv = nc.dram_tensor("Wkv", [D, 2 * DH], F32, kind="ExternalInput")
    Wout = nc.dram_tensor("Wout", [D, D], F32, kind="ExternalInput")
    wT = nc.dram_tensor("wT", [D, D], F32, kind="ExternalInput")
    scol = nc.dram_tensor("scol", [128, KC], F32, kind="ExternalInput")
    bcol = nc.dram_tensor("bcol", [128, KC], F32, kind="ExternalInput")
    cosT = nc.dram_tensor("cosT", [DH, NT], F32, kind="ExternalInput")
    sinT = nc.dram_tensor("sinT", [DH, NT], F32, kind="ExternalInput")
    perm = nc.dram_tensor("perm", [DH, DH], F32, kind="ExternalInput")
    triu = nc.dram_tensor("triu", [128, 128], F32, kind="ExternalInput")
    onescol = nc.dram_tensor("onescol", [128, 1], F32, kind="ExternalInput")
    onesrow = nc.dram_tensor("onesrow", [1, 128], F32, kind="ExternalInput")
    yT = nc.dram_tensor("yT", [D, NT], F32, kind="ExternalOutput")

    S = Sched()
    mk = {}

    with contextlib.ExitStack() as st:
        def sbuf(name, shape, dt):
            return st.enter_context(nc.sbuf_tensor(name, shape, dt))

        A1 = sbuf("A1", [128, KC * NT], BF16)
        A2 = sbuf("A2", [128, KC * NT], BF16)
        KR = sbuf("KR", [128, NT], BF16)
        VS = sbuf("VS", [128, NT], BF16)
        WKV = sbuf("WKV", [128, KC * 256], BF16)
        WS0 = sbuf("WS0", [128, KC * 128], BF16)
        WS1 = sbuf("WS1", [128, KC * 128], BF16)
        ROW0 = sbuf("ROW0", [128, NT], BF16)
        ROW1 = sbuf("ROW1", [128, NT], BF16)
        COSB = sbuf("COSB", [128, NT], BF16)
        SINB = sbuf("SINB", [128, NT], BF16)
        PRM = sbuf("PRM", [128, 128], BF16)
        TRI = sbuf("TRI", [128, 128], BF16)
        ONC = sbuf("ONC", [128, 1], BF16)
        SCB = sbuf("SCB", [128, KC], F32)
        BCB = sbuf("BCB", [128, KC], F32)
        EB = sbuf("EB", [128, 10 * 512], BF16)
        EB2 = sbuf("EB2", [128, 2 * 512], BF16)
        T0 = sbuf("T0", [128, 512], BF16)
        T1 = sbuf("T1", [128, 512], BF16)
        XSQ = sbuf("XSQ", [128, 2 * NT], BF16)
        RB0 = sbuf("RB0", [128, 512], F32)
        RB1 = sbuf("RB1", [128, 512], F32)
        XY0 = sbuf("XY0", [128, 512], F32)
        XY1 = sbuf("XY1", [128, 512], F32)
        MR = sbuf("MR", [1, 512], F32)
        QR = sbuf("QR", [1, 512], F32)
        VE = sbuf("VE", [1, 512], BF16)
        BR = sbuf("BR", [1, 512], BF16)
        ONRB = sbuf("ONRB", [1, 128], BF16)
        PALL = st.enter_context(nc.psum_tensor("PALL", [128, 4096], F32))
        P = [PALL[:, i * 512:(i + 1) * 512] for i in range(8)]
        sem = {k: st.enter_context(nc.semaphore(k)) for k in Sched.SEMS}

        WSTR = (WS0, WS1)
        RB = (RB0, RB1)
        REC = (VE, BR)
        LNR = (MR, QR)
        XY = (XY0, XY1)

        def emit_fill(n, bank, waits=()):
            for i in range(n):
                S.op("tensor",
                     (lambda b: lambda e: e.matmul(
                         P[b][32:33, 0:128], ONC[:], TRI[:],
                         start=False, stop=True))(bank),
                     waits=(waits if i == 0 else []))

        S.op("gpsimd", lambda g: g.dma_start(ONC[:], onescol[:]), incs=[("gq", 16)])
        S.op("gpsimd", lambda g: g.dma_start(ONRB[:], onesrow[:]), incs=[("gq", 16)])
        S.op("gpsimd", lambda g: g.dma_start(PRM[:], perm[:]), incs=[("gq", 16)])
        S.op("gpsimd", lambda g: g.dma_start(TRI[:], triu[:]), incs=[("gq", 16)])
        mk["consts_g"] = S.m("gq")
        mk["onc_g"] = S.m("gq")
        S.op("sync", lambda s: s.dma_start(SCB[:], scol[:]), incs=[("hq", 16)])
        S.op("sync", lambda s: s.dma_start(BCB[:], bcol[:]), incs=[("hq", 16)])
        mk["consts_s"] = S.m("hq")

        xT_r = xT[:].rearrange("(k p) n -> p k n", p=128)
        A1_r = A1[:].rearrange("p (k n) -> p k n", n=NT)
        xt_groups = [(0, 1), (1, 2), (2, 4), (4, 7), (7, 10), (10, 13), (13, 16)]
        xt_of_chunk = {}
        for gidx, (k0, k1) in enumerate(xt_groups):
            sm = "xt" if gidx % 2 == 0 else "xy0"
            S.op("gpsimd",
                 (lambda a, b: lambda e: e.dma_start(A1_r[:, a:b, :], xT_r[:, a:b, :]))(k0, k1),
                 waits=[(sm, S.m(sm))], incs=[(sm, 16)])
            for k in range(k0, k1):
                xt_of_chunk[k] = (sm, S.m(sm))
        mk["xtA_xy0"] = S.m("xy0")
        S.op("gpsimd", lambda g: g.dma_start(COSB[:], cosT[:]), incs=[("gk", 16)])
        S.op("gpsimd", lambda g: g.dma_start(SINB[:], sinT[:]), incs=[("gk", 16)])
        S.op("gpsimd", lambda g: g.dma_start(
            WKV[:].rearrange("p (k c) -> p k c", c=256),
            Wkv[:].rearrange("(k p) c -> p k c", p=128)), incs=[("gk", 16)])
        mk["consts_k"] = S.m("gk")

        mk_sq = {}
        mk_ssq = {}
        for k in range(KC):
            w_sq = [xt_of_chunk[k]]
            if k >= 2:
                w_sq.append(("tq", mk_ssq[k - 2]))
            if k % 2 == 0:
                S.op("scalar",
                     (lambda kk: lambda e: e.activation(
                         XSQ[:, (kk % 2) * NT:(kk % 2) * NT + NT],
                         A1[:, kk * NT:(kk + 1) * NT], AF.Square))(k),
                     waits=w_sq, incs=[("sc", 1)])
                mk_sq[k] = ("sc", S.m("sc"))
            else:
                S.op("vector",
                     (lambda kk: lambda e: e.tensor_mul(
                         XSQ[:, (kk % 2) * NT:(kk % 2) * NT + NT],
                         A1[:, kk * NT:(kk + 1) * NT],
                         A1[:, kk * NT:(kk + 1) * NT]))(k),
                     waits=w_sq, incs=[("ve", 1)])
                mk_sq[k] = ("ve", S.m("ve"))
            for c in range(CC):
                S.op("tensor",
                     (lambda kk, cc: lambda e: e.matmul(
                         P[cc][0:1, :], ONC[:],
                         A1[:, kk * NT + cc * 512: kk * NT + cc * 512 + 512],
                         start=(kk == 0), stop=(kk == KC - 1)))(k, c),
                     waits=([xt_of_chunk[k], ("gq", mk["onc_g"])] if c == 0 else []))
            for c in range(CC):
                S.op("tensor",
                     (lambda kk, cc: lambda e: e.matmul(
                         P[4 + cc][0:1, :], ONC[:],
                         XSQ[:, (kk % 2) * NT + cc * 512: (kk % 2) * NT + cc * 512 + 512],
                         start=(kk == 0), stop=(kk == KC - 1)))(k, c),
                     waits=([mk_sq[k]] if c == 0 else []),
                     incs=([("tq", 1)] if c == CC - 1 else []))
            mk_ssq[k] = S.m("tq")
        mk["stats1_done"] = S.m("tq")

        def emit_ln_rows_and_normalize(src_arena, dst_arena, mk_norm, tag,
                                       chunk_ready_ve=None, abevac_extra=(),
                                       rows_extra=()):
            stats_tq = S.m("tq")
            for c in range(CC):
                w = [("tq", stats_tq)]
                if c > 0:
                    w.append(("tq", mk[f"bcast_{tag}_{c - 1}"]))
                else:
                    w += list(rows_extra)
                S.op("vector",
                     (lambda cc: lambda e: e.tensor_scalar_mul(
                         MR[:], P[cc][0:1, :], 1.0 / D))(c),
                     waits=w, incs=[("ve", 1)])
                S.op("vector",
                     (lambda cc: lambda e: e.tensor_scalar_mul(
                         QR[:], P[4 + cc][0:1, :], 1.0 / D))(c),
                     waits=[("ve", S.m("ve"))], incs=[("ve", 1)])
                mk[f"rowread_{tag}_{c}"] = S.m("ve")
                S.op("vector", lambda e: e.tensor_mul(BR[:], MR[:], MR[:]),
                     waits=[("ve", S.m("ve"))], incs=[("ve", 1)])
                S.op("vector", lambda e: e.scalar_tensor_tensor(
                    VE[:], QR[:], EPS, BR[:], ALU.add, ALU.subtract),
                     waits=[("ve", S.m("ve"))], incs=[("ve", 1)])
                S.op("scalar", lambda e: e.activation(QR[:], VE[:], AF.Ln),
                     waits=[("ve", S.m("ve"))], incs=[("sc", 1)])
                S.op("scalar", lambda e: e.activation(VE[:], QR[:], AF.Exp, scale=-0.5),
                     waits=[("sc", S.m("sc"))], incs=[("sc", 1)])
                mk[f"rstd_{tag}_{c}"] = S.m("sc")
                S.op("vector", lambda e: e.scalar_tensor_tensor(
                    BR[:], MR[:], -1.0, VE[:], ALU.mult, ALU.mult),
                     waits=[("sc", mk[f"rstd_{tag}_{c}"])], incs=[("ve", 1)])
                mk[f"brow_{tag}_{c}"] = S.m("ve")
                S.op("tensor",
                     (lambda cc: lambda e: e.matmul(
                         P[cc][:], ONRB[:], VE[:], start=True, stop=True))(c),
                     waits=[("sc", mk[f"rstd_{tag}_{c}"]),
                            ("ve", mk[f"rowread_{tag}_{c}"]),
                            ("gq", mk["onc_g"])])
                S.op("tensor",
                     (lambda cc: lambda e: e.matmul(
                         P[4 + cc][:], ONRB[:], BR[:], start=True, stop=True))(c),
                     waits=[("ve", mk[f"brow_{tag}_{c}"])], incs=[("tq", 1)])
                mk[f"bcast_{tag}_{c}"] = S.m("tq")
            mk["bcast_" + tag] = S.m("tq")
            ab_waits = [("tq", mk["bcast_" + tag])] + list(abevac_extra)
            for c in range(CC):
                S.op("scalar",
                     (lambda cc: lambda e: e.copy(
                         ROW0[:, cc * 512:cc * 512 + 512], P[cc][:]))(c),
                     waits=(ab_waits if c == 0 else []), incs=[("sc", 1)])
                S.op("scalar",
                     (lambda cc: lambda e: e.copy(
                         ROW1[:, cc * 512:cc * 512 + 512], P[4 + cc][:]))(c),
                     waits=[], incs=[("sc", 1)])
            mk["abevac_" + tag] = S.m("sc")
            for c in range(CC):
                for k in range(KC):
                    w = [("sc", mk["abevac_" + tag])]
                    if chunk_ready_ve is not None:
                        w.append(("ve", chunk_ready_ve[k]))
                    scr = T0 if k % 2 == 0 else T1
                    S.op("vector",
                         (lambda kk, cc, sc_: lambda e: e.tensor_mul(
                             sc_[:],
                             src_arena[:, kk * NT + cc * 512: kk * NT + cc * 512 + 512],
                             ROW0[:, cc * 512:cc * 512 + 512]))(k, c, scr),
                         waits=w, incs=[("ve", 1)])
                    S.op("vector",
                         (lambda kk, cc, sc_: lambda e: e.tensor_add(
                             dst_arena[:, kk * NT + cc * 512: kk * NT + cc * 512 + 512],
                             sc_[:],
                             ROW1[:, cc * 512:cc * 512 + 512]))(k, c, scr),
                         waits=[("ve", S.m("ve"))], incs=[("ve", 1)])
                mk_norm[c] = S.m("ve")

        mk_n1 = {}
        emit_ln_rows_and_normalize(A1, A2, mk_n1, "ln1")

        def emit_rope(dst, dst_off, c, ebuf_off, rot_bank, rot_tq, extra_t1=()):
            S.op("vector",
                 (lambda off, cc: lambda e: e.tensor_mul(
                     T0[:], EB[:, off:off + 512], COSB[:, cc * 512:cc * 512 + 512]))(ebuf_off, c),
                 waits=[("gk", mk["consts_k"]), ("tq", rot_tq), ("ve", S.m("ve"))],
                 incs=[("ve", 1)])
            S.op("vector",
                 (lambda bk, cc: lambda e: e.tensor_mul(
                     T1[:], P[bk][:], SINB[:, cc * 512:cc * 512 + 512]))(rot_bank, c),
                 waits=[("tq", rot_tq)] + list(extra_t1) + [("ve", S.m("ve"))],
                 incs=[("ve", 1)])
            S.op("vector",
                 (lambda d, off: lambda e: e.tensor_add(
                     d[:, off:off + 512], T0[:], T1[:]))(dst, dst_off),
                 waits=[("ve", S.m("ve"))], incs=[("ve", 1)])
            return S.m("ve")

        mk_kr = {}
        mk_qr = {}
        mk_evac = {}
        rot_marks = []

        def flush_rot(pend):
            kind, hh, cc, dst, dst_off, r, evm = pend
            rw = [("sc", evm)]
            if r >= 2:
                rw.append(("ve", rot_marks[r - 2]))
            S.op("tensor",
                 (lambda rr: lambda e: e.matmul(
                     P[2 + rr % 2][:], PRM[:],
                     EB[:, (rr % 2) * 512:(rr % 2) * 512 + 512],
                     start=True, stop=True))(r),
                 waits=rw, incs=[("tq", 1)])
            rot_tq = S.m("tq")
            extra = [("tq", mk["stats1_done"])] if (kind == "q" and hh == 0 and cc == 0) else []
            m = emit_rope(dst, dst_off, cc, (r % 2) * 512, 2 + r % 2,
                          rot_tq, extra_t1=extra)
            rot_marks.append(m)
            if kind == "k":
                mk_kr[cc] = m
            else:
                mk_qr[(hh, cc)] = m

        pendB = None
        evac_pg = []
        gi = 0
        rgi = 0
        for c in range(CC):
            for kind, idx in [("k", c)] + [("v", jt) for jt in range(4 * c, 4 * c + 4)]:
                par = gi % 2
                if kind == "k":
                    w = [("gk", mk["consts_k"]), ("ve", mk_n1[idx])]
                else:
                    w = [("ve", mk_n1[idx // 4])]
                if gi < 2:
                    w.append(("sc", mk["abevac_ln1"]))
                else:
                    w.append(("sc", evac_pg[gi - 2]))
                for k in range(KC):
                    if kind == "k":
                        fn = (lambda kk, cc, pp: lambda e: e.matmul(
                            P[pp][:], WKV[:, kk * 256:kk * 256 + 128],
                            A2[:, kk * NT + cc * 512: kk * NT + cc * 512 + 512],
                            start=(kk == 0), stop=(kk == KC - 1)))(k, idx, par)
                    else:
                        fn = (lambda kk, jj, pp: lambda e: e.matmul(
                            P[pp][:, 0:128],
                            A2[:, kk * NT + jj * 128: kk * NT + jj * 128 + 128],
                            WKV[:, kk * 256 + 128:kk * 256 + 256],
                            start=(kk == 0), stop=(kk == KC - 1)))(k, idx, par)
                    S.op("tensor", fn, waits=(w if k == 0 else []),
                         incs=([("tq", 1)] if k == KC - 1 else []))
                mk[("proj", kind, idx)] = S.m("tq")
                if pendB is not None:
                    flush_rot(pendB)
                    pendB = None
                if kind == "k":
                    ew = [("tq", mk[("proj", kind, idx)])]
                    if rgi >= 2:
                        ew.append(("ve", rot_marks[rgi - 2]))
                    S.op("scalar",
                         (lambda rr, pp: lambda e: e.copy(
                             EB[:, (rr % 2) * 512:(rr % 2) * 512 + 512], P[pp][:]))(rgi, par),
                         waits=ew, incs=[("sc", 1)])
                    pendB = ("k", None, idx, KR, idx * 512, rgi, S.m("sc"))
                    rgi += 1
                else:
                    S.op("scalar",
                         (lambda jj, pp: lambda e: e.copy(
                             VS[:, jj * 128:jj * 128 + 128], P[pp][:, 0:128]))(idx, par),
                         waits=[("tq", mk[("proj", kind, idx)])], incs=[("sc", 1)])
                evac_pg.append(S.m("sc"))
                gi += 1

        mk_qp = {}
        for h in range(H):
            wbuf = WSTR[h % 2]
            w = [("hq", mk["consts_s"])]
            if h >= 2:
                w.append(("tq", mk_qp[(h - 2, CC - 1)]))
            S.op("gpsimd",
                 (lambda hh, wb: lambda e: e.dma_start(
                     wb[:], Wq[hh * 128:hh * 128 + 128, :]))(h, wbuf),
                 waits=w, incs=[(f"wq{h % 2}", 16)])
            mk[("wq", h)] = S.m(f"wq{h % 2}")
            for c in range(CC):
                par = gi % 2
                w = [(f"wq{h % 2}", mk[("wq", h)]), ("sc", evac_pg[gi - 2])]
                for k in range(KC):
                    S.op("tensor",
                         (lambda kk, hh, cc, pp, wb: lambda e: e.matmul(
                             P[pp][:], wb[:, kk * 128:kk * 128 + 128],
                             A2[:, kk * NT + cc * 512: kk * NT + cc * 512 + 512],
                             start=(kk == 0), stop=(kk == KC - 1)))(k, h, c, par, wbuf),
                         waits=(w if k == 0 else []),
                         incs=([("tq", 1)] if k == KC - 1 else []))
                mk_qp[(h, c)] = S.m("tq")
                if pendB is not None:
                    flush_rot(pendB)
                ew = [("tq", mk_qp[(h, c)]), ("ve", rot_marks[rgi - 2])]
                S.op("scalar",
                     (lambda rr, pp: lambda e: e.copy(
                         EB[:, (rr % 2) * 512:(rr % 2) * 512 + 512], P[pp][:]))(rgi, par),
                     waits=ew, incs=[("sc", 1)])
                mk_evac[("q", h, c)] = S.m("sc")
                evac_pg.append(S.m("sc"))
                pendB = ("q", h, c, A1, h * NT + c * 512, rgi, S.m("sc"))
                rgi += 1
                gi += 1
        flush_rot(pendB)
        pendB = None
        mk["krope_done"] = mk_kr[CC - 1]
        mk["phaseB_t_end"] = S.m("tq")
        mk["rope_all_done"] = S.m("ve")

        mk_expp = {}
        mk_mask = {}
        mk_ereader = {}
        mk_dln = {}
        mk_rbc = {}
        mk_rbcast = {}
        mk_ott = {}

        def emit_sim(h, c, idx, si):
            u0 = max(0, (idx - 4 * c) * 128)
            wdt = 512 - u0
            w = [("ve", mk_qr[(h, c)]), ("ve", mk["krope_done"])]
            if si < 4:
                w.append(("sc", mk_evac[("q", H - 1, CC - 1)]))
                w.append(("ve", mk["rope_all_done"]))
            else:
                w.append(("sc", mk_expp[(si - 4) // 2]))
            S.op("tensor",
                 (lambda ps, hh, cc, jj, uu, ww: lambda e: e.matmul(
                     P[ps][:, uu:512], KR[:, jj * 128:jj * 128 + 128],
                     A1[:, hh * NT + cc * 512 + uu: hh * NT + cc * 512 + uu + ww],
                     start=True, stop=True))(si % 4, h, c, idx, u0, wdt),
                 waits=w, incs=[("tq", 1)])
            mk[("sim", si)] = S.m("tq")

        def emit_expp(h, c, pi, si0):
            ps = si0 % 4
            eb0 = si0 % 10
            u0 = max(0, (2 * pi - 4 * c) * 128)
            w = [("tq", mk[("sim", si0 + 1)])]
            if si0 >= 10:
                w.append(("tq", mk_ereader[si0 - 9]))
            else:
                w.append(("tq", mk["phaseB_t_end"]))
                w.append(("ve", mk["rope_all_done"]))
            S.op("scalar",
                 (lambda sl, ee, uu: lambda e: e.activation(
                     EB[:, ee * 512 + uu:ee * 512 + 1024],
                     PALL[:, sl * 512 + uu:sl * 512 + 1024], AF.Exp, scale=SCALE))(ps, eb0, u0),
                 waits=w, incs=[("sc", 1)])
            mk_expp[si0 // 2] = S.m("sc")
            for si in (si0, si0 + 1):
                idx = si - si0 + (pi * 2)
                if idx >= 4 * c:
                    eb = si % 10
                    uu = (idx - 4 * c) * 128
                    S.op("vector",
                         (lambda ee, u_: lambda e: e.tensor_mul(
                             EB[:, ee * 512 + u_:ee * 512 + u_ + 128],
                             EB[:, ee * 512 + u_:ee * 512 + u_ + 128], TRI[:]))(eb, uu),
                         waits=[("sc", mk_expp[si0 // 2]), ("gq", mk["consts_g"])],
                         incs=[("ve", 1)])
                    mk_mask[si] = S.m("ve")

        e2_state = {"n": 0}
        mk_e2den = {}

        def emit_O(h, c, idx, si, par, g, njt, with_ew):
            u0 = max(0, (idx - 4 * c) * 128)
            wdt = 512 - u0
            eb = si % 10
            w0 = []
            if with_ew:
                w0 += ([("ve", mk_mask[si])] if si in mk_mask
                       else [("sc", mk_expp[si // 2])])
            if idx == 0:
                w0.append(("sc", mk["abevac_ln1"]) if g < 2 else ("ve", mk_ott[g - 2]))
            S.op("tensor",
                 (lambda pp, ee, jj, uu, ww, st_, sp_: lambda e: e.matmul(
                     P[4 + pp][:, uu:512], VS[:, jj * 128:jj * 128 + 128],
                     EB[:, ee * 512 + uu:ee * 512 + 512],
                     start=st_, stop=sp_))(par, eb, idx, u0, wdt, idx == 0, idx == njt - 1),
                 waits=w0, incs=[("tq", 1)])
            mk_ereader[si] = S.m("tq")
            if idx == njt - 1:
                mk[("odone", h, c)] = S.m("tq")

        def emit_den(h, c, idx, si, par, g, njt, first_den, last_den, with_ew):
            u0 = max(0, (idx - 4 * c) * 128)
            wdt = 512 - u0
            eb = si % 10
            w0 = []
            if with_ew:
                w0 += ([("ve", mk_mask[si])] if si in mk_mask
                       else [("sc", mk_expp[si // 2])])
            if first_den:
                w0.append(("sc", mk["abevac_ln1"]) if g < 2 else ("ve", mk_rbc[g - 2]))
            S.op("tensor",
                 (lambda pp, ee, uu, ww, st_, sp_: lambda e: e.matmul(
                     P[6 + pp][0:1, uu:512], ONC[:],
                     EB[:, ee * 512 + uu:ee * 512 + 512],
                     start=st_, stop=sp_))(par, eb, u0, wdt, first_den, last_den),
                 waits=w0,
                 incs=([("tq", 1)] if last_den else []))
            if last_den:
                mk[("dend", h, c)] = S.m("tq")

        def emit_e2(h, c, si0, par, g, first_den):
            n = e2_state["n"]
            slot = n % 2
            e2_state["n"] = n + 1
            w = [("sc", mk_expp[si0 // 2]), ("ve", S.m("ve"))]
            if n >= 2:
                w.append(("tq", mk_e2den[n - 2]))
            eb = si0 % 10
            S.op("vector",
                 (lambda sl, ee: lambda e: e.tensor_add(
                     EB2[:, sl * 512:sl * 512 + 512],
                     EB[:, ee * 512:ee * 512 + 512],
                     EB[:, ee * 512 + 512:ee * 512 + 1024]))(slot, eb),
                 waits=w, incs=[("ve", 1)])
            return (n, slot, S.m("ve"), par, g, first_den, si0)

        def flush_pair_den(pd):
            n, slot, ve_mark, par, g, first_den, si0 = pd
            w = [("ve", ve_mark)]
            if first_den:
                w.append(("sc", mk["abevac_ln1"]) if g < 2 else ("ve", mk_rbc[g - 2]))
            S.op("tensor",
                 (lambda pp, sl, st_: lambda e: e.matmul(
                     P[6 + pp][0:1, :], ONC[:],
                     EB2[:, sl * 512:sl * 512 + 512],
                     start=st_, stop=False))(par, slot, first_den),
                 waits=w, incs=[("tq", 1)])
            mk_e2den[n] = S.m("tq")
            mk_ereader[si0] = mk_e2den[n]
            mk_ereader[si0 + 1] = mk_e2den[n]

        def tail_A(h, c, par, g):
            w = [("tq", mk[("dend", h, c)])]
            if g == 0:
                w.append(("ve", mk["brow_ln1_3"]))
            S.op("scalar",
                 (lambda pp: lambda e: e.activation(
                     LNR[pp][:], P[6 + pp][0:1, :], AF.Ln))(par),
                 waits=w, incs=[("sc", 1)])
            w = [("sc", S.m("sc"))]
            if g >= 2:
                w.append(("tq", mk_rbcast[g - 2]))
            else:
                w.append(("tq", mk["bcast_ln1"]))
            S.op("scalar",
                 (lambda pp: lambda e: e.activation(
                     REC[pp][:], LNR[pp][:], AF.Exp, scale=-1.0))(par),
                 waits=w, incs=[("sc", 1)])
            mk_dln[g] = S.m("sc")
            w = [("sc", mk_dln[g])]
            if g < 2:
                w.append(("sc", mk["abevac_ln1"]))
            S.op("tensor",
                 (lambda pp: lambda e: e.matmul(
                     P[6 + pp][:], ONRB[:], REC[pp][:], start=True, stop=True))(par),
                 waits=w, incs=[("tq", 1)])
            mk_rbcast[g] = S.m("tq")

        def tail_B(h, c, par, g):
            w = [("tq", mk_rbcast[g])]
            if g >= 2:
                w.append(("ve", mk_ott[g - 2]))
            S.op("vector",
                 (lambda pp: lambda e: e.tensor_copy(RB[pp][:], P[6 + pp][:]))(par),
                 waits=w, incs=[("ve", 1)])
            mk_rbc[g] = S.m("ve")
            w = [("tq", mk[("odone", h, c)]), ("ve", mk_rbc[g])]
            if g == 0:
                w.append(("tq", mk["phaseB_t_end"]))
            S.op("vector",
                 (lambda pp, hh, cc: lambda e: e.tensor_mul(
                     A2[:, hh * NT + cc * 512: hh * NT + cc * 512 + 512],
                     RB[pp][:], P[4 + pp][:]))(par, h, c),
                 waits=w, incs=[("ve", 1)])
            mk_ott[g] = S.m("ve")

        tiles = []
        for h in range(H):
            for c in range(CC):
                g = CC * h + c
                njt = 4 * (c + 1)
                for idx in range(njt):
                    tiles.append((h, c, idx, g % 2, g, njt))
        n_t = len(tiles)
        n_p = n_t // 2

        def sim_at(si):
            h, c, idx, par, g, njt = tiles[si]
            emit_sim(h, c, idx, si)

        def expp_at(p):
            h, c, idx, par, g, njt = tiles[2 * p]
            emit_expp(h, c, idx // 2, 2 * p)

        for p in (0, 1, 2, 3):
            sim_at(2 * p)
            sim_at(2 * p + 1)
            if p < 2:
                expp_at(p)
        pendC = None
        pend_den = None
        pend_age = 0
        for pi in range(n_p):
            h, c, idx, par, g, njt = tiles[2 * pi]
            si0 = 2 * pi
            if pi + 2 < n_p:
                expp_at(pi + 2)
            if 2 * pi + 8 < n_t:
                sim_at(2 * pi + 8)
                sim_at(2 * pi + 9)
            if pendC is not None:
                pend_age += 1
                if pend_age == 1:
                    tail_A(*pendC)
                elif pend_age == 2:
                    tail_B(*pendC)
                    pendC = None
            if idx >= 4 * c:
                if pend_den is not None:
                    flush_pair_den(pend_den)
                    pend_den = None
                emit_den(h, c, idx, si0, par, g, njt, idx == 0, False, True)
                emit_O(h, c, idx, si0, par, g, njt, False)
                emit_den(h, c, idx + 1, si0 + 1, par, g, njt, False,
                         idx + 1 == njt - 1, True)
                emit_O(h, c, idx + 1, si0 + 1, par, g, njt, False)
            else:
                if pend_den is not None:
                    flush_pair_den(pend_den)
                emit_O(h, c, idx, si0, par, g, njt, True)
                emit_O(h, c, idx + 1, si0 + 1, par, g, njt, True)
                pend_den = emit_e2(h, c, si0, par, g, idx == 0)
            if idx + 1 == njt - 1:
                pendC = (h, c, par, g)
                pend_age = 0
        tail_A(*pendC)
        tail_B(*pendC)
        pendC = None
        mk["attn_t_end"] = S.m("tq")
        mk["exp_last"] = mk_expp[n_p - 1]

        mk_res = {}
        mk_xt = {}
        mk_wmm = {}
        for dt in range(KC):
            if dt < 2:
                w = [("tq", mk_qp[(H - 1, CC - 1)])]
            else:
                w = [("tq", mk_wmm[(dt - 2, CC - 1)])]
            S.op("gpsimd",
                 (lambda dd, wb: lambda e: e.dma_start(
                     wb[:], Wout[dd * 128:dd * 128 + 128, :]))(dt, WSTR[dt % 2]),
                 waits=w, incs=[(f"wq{dt % 2}", 16)])
            mk[("wo", dt)] = S.m(f"wq{dt % 2}")
            for c in range(CC):
                par = (CC * dt + c) % 4
                gD = CC * dt + c
                w = []
                if gD >= 2:
                    w.append(("ve", mk_res[(dt, c - 2) if c >= 2 else (dt - 1, c + 2)]))
                elif c % 2 == 0:
                    w.append(("xy0", mk["xtA_xy0"]))
                S.op("sync",
                     (lambda dd, cc: lambda e: e.dma_start(
                         XY[cc % 2][:], xT[dd * 128:dd * 128 + 128,
                                           cc * 512:cc * 512 + 512]))(dt, c),
                     waits=w, incs=[(f"xy{c % 2}", 16)])
                mk_xt[(dt, c)] = S.m(f"xy{c % 2}")
                w = [(f"wq{dt % 2}", mk[("wo", dt)]), ("ve", mk_ott[CC * (H - 1) + c])]
                if gD < 4:
                    w.append(("sc", mk["exp_last"]))
                else:
                    w.append(("ve", mk_res[(dt - 1, c)]))
                for hh in range(H):
                    S.op("tensor",
                         (lambda kk, dd, cc, pp, wb: lambda e: e.matmul(
                             P[pp][:], wb[:, kk * 128:kk * 128 + 128],
                             A2[:, kk * NT + cc * 512: kk * NT + cc * 512 + 512],
                             start=(kk == 0), stop=(kk == H - 1)))(hh, dt, c, par, WSTR[dt % 2]),
                         waits=(w if hh == 0 else []),
                         incs=([("tq", 1)] if hh == H - 1 else []))
                mk_wmm[(dt, c)] = S.m("tq")
                w = [("tq", mk_wmm[(dt, c)]), (f"xy{c % 2}", mk_xt[(dt, c)])]
                if gD == 0:
                    w.append(("tq", mk["attn_t_end"]))
                S.op("vector",
                     (lambda dd, cc, pp: lambda e: e.tensor_add(
                         A1[:, dd * NT + cc * 512: dd * NT + cc * 512 + 512],
                         XY[cc % 2][:], P[pp][:]))(dt, c, par),
                     waits=w, incs=[("ve", 1)])
                mk_res[(dt, c)] = S.m("ve")
        mk["phaseD_t_end"] = S.m("tq")

        mk_sq2 = {}
        mk_ssq2 = {}
        for k in range(KC):
            w_sq = [("ve", mk_res[(k, CC - 1)])]
            if k >= 2:
                w_sq.append(("tq", mk_ssq2[k - 2]))
            if k % 2 == 0:
                S.op("scalar",
                     (lambda kk: lambda e: e.activation(
                         XSQ[:, (kk % 2) * NT:(kk % 2) * NT + NT],
                         A1[:, kk * NT:(kk + 1) * NT], AF.Square))(k),
                     waits=w_sq, incs=[("sc", 1)])
                mk_sq2[k] = ("sc", S.m("sc"))
            else:
                S.op("vector",
                     (lambda kk: lambda e: e.tensor_mul(
                         XSQ[:, (kk % 2) * NT:(kk % 2) * NT + NT],
                         A1[:, kk * NT:(kk + 1) * NT],
                         A1[:, kk * NT:(kk + 1) * NT]))(k),
                     waits=w_sq, incs=[("ve", 1)])
                mk_sq2[k] = ("ve", S.m("ve"))
            for c in range(CC):
                w = []
                if k == 0 and c == 0:
                    w = [("ve", mk_res[(KC - 1, CC - 1)])]
                S.op("tensor",
                     (lambda kk, cc: lambda e: e.matmul(
                         P[cc][0:1, :], ONC[:],
                         A1[:, kk * NT + cc * 512: kk * NT + cc * 512 + 512],
                         start=(kk == 0), stop=(kk == KC - 1)))(k, c),
                     waits=w)
            for c in range(CC):
                S.op("tensor",
                     (lambda kk, cc: lambda e: e.matmul(
                         P[4 + cc][0:1, :], ONC[:],
                         XSQ[:, (kk % 2) * NT + cc * 512: (kk % 2) * NT + cc * 512 + 512],
                         start=(kk == 0), stop=(kk == KC - 1)))(k, c),
                     waits=([mk_sq2[k]] if c == 0 else []),
                     incs=([("tq", 1)] if c == CC - 1 else []))
            mk_ssq2[k] = S.m("tq")

        mk_n2 = {}
        emit_ln_rows_and_normalize(
            A1, A2, mk_n2, "ln2",
            abevac_extra=[("ve", mk["rope_all_done"])],
            rows_extra=[("sc", mk_dln[CC * H - 1])])

        mk_yv = {}
        mk_ydma = {}
        mk_gmm = {}
        mk_gact = {}
        order = [(o, c) for c in range(CC) for o in (0, 1)] + \
                [(o, c) for o in range(2, KC) for c in range(CC)]
        for seq, (ot, c) in enumerate(order):
            if c == 0 or (ot < 2 and seq == ot):
                if ot < 2:
                    w = [("tq", mk_wmm[(KC - 2 + ot, CC - 1)])]
                else:
                    w = [("tq", mk_gmm[(ot - 2, CC - 1)])]
                S.op("gpsimd",
                     (lambda oo, wb: lambda e: e.dma_start(
                         wb[:], wT[oo * 128:oo * 128 + 128, :]))(ot, WSTR[ot % 2]),
                     waits=w, incs=[(f"wq{ot % 2}", 16)])
                mk[("wf", ot)] = S.m(f"wq{ot % 2}")
            par = seq % 4
            w = [(f"wq{ot % 2}", mk[("wf", ot)]), ("ve", mk_n2[c])]
            if seq >= 4:
                w.append(("sc", mk_gact[order[seq - 4]]))
            for k in range(KC):
                S.op("tensor",
                     (lambda kk, oo, cc, pp, wb: lambda e: e.matmul(
                         P[pp][:], wb[:, kk * 128:kk * 128 + 128],
                         A2[:, kk * NT + cc * 512: kk * NT + cc * 512 + 512],
                         start=(kk == 0), stop=(kk == KC - 1)))(k, ot, c, par, WSTR[ot % 2]),
                     waits=(w if k == 0 else []),
                     incs=([("tq", 1)] if k == KC - 1 else []))
            mk_gmm[(ot, c)] = S.m("tq")
            w = [("tq", mk_gmm[(ot, c)])]
            if seq < 2:
                w.append(("ve", mk_ott[CC * H - 1]))
            else:
                w.append(("ve", mk_yv[order[seq - 2]]))
            S.op("scalar",
                 (lambda oo, cc, rs, pp: lambda e: e.activation(
                     RB[rs][:], P[pp][:], AF.Identity,
                     bias=BCB[:, oo:oo + 1], scale=SCB[:, oo:oo + 1]))(ot, c, seq % 2, par),
                 waits=w, incs=[("sc", 1)])
            mk_gact[(ot, c)] = S.m("sc")
            w = [("sc", mk_gact[(ot, c)])]
            if seq >= 2:
                pc2 = order[seq - 2]
                w.append((f"xy{(seq - 2) % 2}", mk_ydma[pc2]))
            S.op("vector",
                 (lambda oo, cc, rs, xs: lambda e: e.tensor_add(
                     XY[xs][:], RB[rs][:],
                     A1[:, oo * NT + cc * 512: oo * NT + cc * 512 + 512]))(ot, c, seq % 2, seq % 2),
                 waits=w, incs=[("ve", 1)])
            mk_yv[(ot, c)] = S.m("ve")
            S.op("sync",
                 (lambda oo, cc, xs: lambda e: e.dma_start(
                     yT[oo * 128:oo * 128 + 128, cc * 512:cc * 512 + 512],
                     XY[xs][:]))(ot, c, seq % 2),
                 waits=[("ve", mk_yv[(ot, c)])], incs=[(f"xy{seq % 2}", 16)])
            mk_ydma[(ot, c)] = S.m(f"xy{seq % 2}")

        final_xy0 = S.m("xy0")
        final_xy1 = S.m("xy1")
        S.op("sync", lambda e: e.wait_ge(sem["xy0"], final_xy0))
        S.op("sync", lambda e: e.wait_ge(sem["xy1"], final_xy1))

        with nc.Block() as block:
            def runner(ename):
                def run(eng):
                    last = {}
                    for waits, fn, incs in S.ops[ename]:
                        merged = {}
                        for s, v in waits:
                            merged[s] = max(merged.get(s, 0), v)
                        for s, v in merged.items():
                            if v > 0 and last.get(s, -1) < v:
                                eng.wait_ge(sem[s], v)
                                last[s] = v
                        inst = fn(eng)
                        for s, a in incs:
                            inst.then_inc(sem[s], a)
                return run
            block.tensor(runner("tensor"))
            block.scalar(runner("scalar"))
            block.vector(runner("vector"))
            block.sync(runner("sync"))
            block.gpsimd(runner("gpsimd"))

    return nc


def _host_tables():
    inv_freq = (1.0 / (np.float32(10000.0) **
                       (np.arange(0, DH, 2, dtype=np.float32) / np.float32(DH)))).astype(np.float32)
    freqs = np.arange(NT, dtype=np.float32)[:, None] * inv_freq[None, :]
    ang = np.concatenate([freqs, freqs], axis=1)
    cosT = np.ascontiguousarray(np.cos(ang.astype(np.float64)).astype(np.float32).T)
    sinT = np.ascontiguousarray(np.sin(ang.astype(np.float64)).astype(np.float32).T)
    perm = np.zeros((DH, DH), dtype=np.float32)
    for d in range(DH):
        if d < 64:
            perm[d + 64, d] = -1.0
        else:
            perm[d - 64, d] = 1.0
    r = np.arange(128)
    triu = (r[None, :] >= r[:, None]).astype(np.float32)
    return cosT, sinT, perm, triu


def _pretile(m):
    return np.ascontiguousarray(
        m.reshape(KC, 128, KC, 128).transpose(2, 1, 0, 3).reshape(D, D))


def kernel(L0_attn_x, L0_attn_Wq, L0_attn_Wkv, L0_attn_Wout,
           L0_ffn_weight, L0_ffn_scale_weight, L0_ffn_bias):
    x = np.asarray(L0_attn_x, dtype=np.float32)
    Wq_h = _pretile(np.asarray(L0_attn_Wq, dtype=np.float32))
    Wkv_h = np.ascontiguousarray(np.asarray(L0_attn_Wkv, dtype=np.float32))
    Wout_h = _pretile(np.asarray(L0_attn_Wout, dtype=np.float32))
    wT_h = _pretile(np.asarray(L0_ffn_weight, dtype=np.float32).T)
    scol = np.ascontiguousarray(
        np.asarray(L0_ffn_scale_weight, dtype=np.float32).reshape(KC, 128).T)
    bcol = np.ascontiguousarray(
        np.asarray(L0_ffn_bias, dtype=np.float32).reshape(KC, 128).T)
    cosT_h, sinT_h, perm_h, triu_h = _host_tables()
    onescol = np.ones((128, 1), dtype=np.float32)
    onesrow = np.ones((1, 128), dtype=np.float32)

    nc = build_nc()
    in_maps = []
    for b in range(B):
        in_maps.append({
            "xT": np.ascontiguousarray(x[b].T),
            "Wq": Wq_h, "Wkv": Wkv_h, "Wout": Wout_h, "wT": wT_h,
            "scol": scol, "bcol": bcol,
            "cosT": cosT_h, "sinT": sinT_h, "perm": perm_h, "triu": triu_h,
            "onescol": onescol, "onesrow": onesrow,
        })
    res = run_bass_kernel_spmd(nc, in_maps, core_ids=list(range(B)))
    out = np.empty((B, NT, D), dtype=np.float32)
    for b, r in enumerate(res.results):
        out[b] = r["yT"].T
    return out
